# version 08
# baseline (speedup 1.0000x reference)
import numpy as np
from contextlib import ExitStack

import concourse.bass as bass
import concourse.mybir as mybir
from concourse.bass_utils import run_bass_kernel_spmd

F32 = mybir.dt.float32
BF16 = mybir.dt.bfloat16
AF = mybir.ActivationFunctionType
ALU = mybir.AluOpType
AX = mybir.AxisListType

D = 1024
NCH = 8
S = 8192
OWN = 4096
NT = 512
NQ = 1024
NQRT = 4
NE = 8
EPS_LN = 1e-5
EPS_L2 = 1e-12


def build_nc():
    nc = bass.Bass("TRN2", target_bir_lowering=False, debug=False)
    hx = nc.dram_tensor("x", [D, S], F32, kind="ExternalInput").ap()
    hwq = nc.dram_tensor("wq", [D, D], F32, kind="ExternalInput").ap()
    hwk = nc.dram_tensor("wk", [D, D], F32, kind="ExternalInput").ap()
    hwv = nc.dram_tensor("wv", [D, D], F32, kind="ExternalInput").ap()
    hgate = nc.dram_tensor("gate", [D, NE], F32, kind="ExternalInput").ap()
    hw12 = nc.dram_tensor("w12", [NE, 2, D, 2048], F32, kind="ExternalInput").ap()
    hw3 = nc.dram_tensor("w3", [NE, 2, D, D], F32, kind="ExternalInput").ap()
    hout = nc.dram_tensor("out", [D, OWN], F32, kind="ExternalOutput").ap()
    hgin = nc.dram_tensor("gfin", [128, NCH], F32, kind="Internal")
    hgout = nc.dram_tensor("gfout", [128, NCH], F32, kind="Internal")

    with ExitStack() as ctx:
        en = ctx.enter_context
        xb = [en(nc.sbuf_tensor(f"xb{i}", [128, NCH * NT], BF16)) for i in range(2)]
        hb = [en(nc.sbuf_tensor(f"hb{i}", [128, NCH * NT], BF16)) for i in range(2)]
        sq = [en(nc.sbuf_tensor(f"sq{i}", [128, NCH * NT], BF16)) for i in range(2)]
        qk = en(nc.sbuf_tensor("qk", [128, NCH * NT], BF16))
        rsqbS = en(nc.sbuf_tensor("rsqbS", [128, NT], BF16))
        rows = en(nc.sbuf_tensor("rows", [1, 2560], F32))
        rowsb = en(nc.sbuf_tensor("rowsb", [1, 1536], BF16))
        onesc = en(nc.sbuf_tensor("onesc", [128, 1], BF16))
        onesr = en(nc.sbuf_tensor("onesr", [1, 128], BF16))
        ident = en(nc.sbuf_tensor("ident", [128, 128], BF16))
        pidx = en(nc.sbuf_tensor("pidx", [128, 1], F32))
        krep = en(nc.sbuf_tensor("krep", [8, 128], F32))
        onehot = en(nc.sbuf_tensor("onehot", [8, NE * 128], BF16))
        gfacc = en(nc.sbuf_tensor("gfacc", [128, NCH], F32))
        gft = en(nc.sbuf_tensor("gft", [128, NCH], F32))
        gfsb = en(nc.sbuf_tensor("gfsb", [128, NCH], F32))
        gatesb = en(nc.sbuf_tensor("gatesb", [128, NCH * NE], BF16))
        probs = en(nc.sbuf_tensor("probs", [128, 64], BF16))
        pr8 = en(nc.sbuf_tensor("pr8", [8, NQ], BF16))
        pbS = en(nc.sbuf_tensor("pbS", [128, NQ], BF16))
        colsc = en(nc.sbuf_tensor("colsc", [128, 16], F32))
        scr8 = en(nc.sbuf_tensor("scr8", [128, 4 * NE], F32))
        tmps = [en(nc.sbuf_tensor("tmps0", [128, NT], BF16))] * 2
        tmpe = en(nc.sbuf_tensor("tmpe", [128, NT], BF16))
        epsc = en(nc.sbuf_tensor("epsc", [128, 1], F32))
        wregion = en(nc.sbuf_tensor("wregion", [128, 16384], BF16))
        wk_sb = wregion[:, 0:8192]
        wv_sb = wregion[:, 8192:16384]
        w12sb = [wregion[:, :], en(nc.sbuf_tensor("w12B", [128, 16384], BF16))[:, :]]
        w3wq = en(nc.sbuf_tensor("w3wq", [128, 8192], BF16))
        ysb = en(nc.sbuf_tensor("ysb", [128, NCH * NQ], BF16))
        h2sb = en(nc.sbuf_tensor("h2sb", [128, NCH * NQ], BF16))
        hid = [en(nc.sbuf_tensor(f"hid{i}", [128, 8 * NT], BF16)) for i in range(2)]

        pstat = en(nc.psum_tensor("pstat", [65, 512], F32))
        pbt = [en(nc.psum_tensor(f"pb{i}", [128, 512], F32)) for i in range(1, 8)]
        pb1, pb2, pb3, pb4, pb5, pb6, pb7 = pbt
        PB = {"pb1": pb1, "pb2": pb2, "pb3": pb3, "pb4": pb4, "pb5": pb5,
              "pb6": pb6, "pb7": pb7}
        ptr = pb7[0:8, 64:128].bitcast(BF16)

        sem_names = ["pe", "dv", "ac", "gp", "x0", "x1", "w0", "w1", "w3", "c", "out"]
        SEM = {k: en(nc.semaphore(f"s_{k}")) for k in sem_names}
        C = {k: 0 for k in sem_names}
        DMAKEYS = ("x0", "x1", "w0", "w1", "w3", "c", "out")

        block = en(nc.Block())

        def inc(ins, key):
            n = 16 if key in DMAKEYS else 1
            ins.then_inc(SEM[key], n)
            C[key] += n
            return C[key]

        def w(eng, key, val):
            assert val <= C[key], (key, val, C[key])
            if val > 0:
                eng.wait_ge(SEM[key], val)

        bankf = {}

        def bank_wait(te, bank):
            for k, v in bankf.get(bank, {}).items():
                w(te, k, v)
            bankf[bank] = {}

        def bank_read(bank, key, cnt):
            d = bankf.setdefault(bank, {})
            d[key] = max(d.get(key, 0), cnt)

        T, V, A, G = block.tensor, block.vector, block.scalar, block.gpsimd
        st = {"dv_rsqbS_reads": 0}

        def _(g):
            g.memset(onesc[:, :], 1.0)
            g.memset(onesr[:, :], 1.0)
            g.memset(gfacc[:, :], 0.0)
            g.memset(epsc[:, :], EPS_LN)
            g.iota(pidx[:, :], [[0, 1]], channel_multiplier=1,
                   allow_small_or_imprecise_dtypes=True)
            g.iota(krep[:, :], [[0, 128]], channel_multiplier=1,
                   allow_small_or_imprecise_dtypes=True)
            cidx = sq[0][:, 0:256].bitcast(F32)
            ins = g.iota(cidx, [[1, 128]], channel_multiplier=0,
                         allow_small_or_imprecise_dtypes=True)
            st["gp_consts"] = inc(ins, "gp")
        G(_)

        def _(v):
            w(v, "gp", st["gp_consts"])
            cidx = sq[0][:, 0:256].bitcast(F32)
            ins = v.tensor_scalar(ident[:, :], cidx, pidx[:, :], None, ALU.is_equal)
            for e in range(NE):
                ins = v.tensor_scalar(onehot[:, e * 128:(e + 1) * 128], krep[:, :],
                                      float(e), None, ALU.is_equal)
            st["dv_consts"] = inc(ins, "dv")
        V(_)

        def _(g):
            ins = g.dma_start(wk_sb.rearrange("p (k m) -> p k m", k=NCH),
                              hwk.rearrange("(k p) m -> p k m", p=128))
            inc(ins, "c")
            ins = g.dma_start(wv_sb.rearrange("p (k m) -> p k m", k=NCH),
                              hwv.rearrange("(k p) m -> p k m", p=128))
            inc(ins, "c")
            ins = g.dma_start(gatesb[:].rearrange("p (k e) -> p k e", k=NCH),
                              hgate.rearrange("(k p) e -> p k e", p=128))
            inc(ins, "c")
            ins = g.dma_start(w3wq[:].rearrange("p (k m) -> p k m", k=NCH),
                              hwq.rearrange("(k p) m -> p k m", p=128))
            st["c_init"] = inc(ins, "c")
        G(_)

        xfree = [("pe", 0), ("pe", 0)]
        hfree = [("pe", 0), ("pe", 0)]
        sqfree = [[("dv", 0)], [("dv", 0)]]
        rowfreeLN = {}
        rowfreeL2 = {}
        xcnt = [0, 0]

        def emit_ln(par, colf, src_wait, dst_colf, dst_wait, flat_src=None):


            def _(a):
                for k, v in src_wait:
                    w(a, k, v)
                for k, v in sqfree[par]:
                    w(a, k, v)
                if flat_src is not None:
                    ins = a.activation(sq[par][:, :], flat_src, AF.Square)
                else:
                    for c in range(NCH):
                        ins = a.activation(sq[par][:, c * NT:(c + 1) * NT], colf(c), AF.Square)
                st["ac_sq"] = inc(ins, "ac")
            A(_)
            ac_sq = st["ac_sq"]

            def _(te):
                for k, v in src_wait:
                    w(te, k, v)
                w(te, "ac", ac_sq)
                w(te, "gp", st["gp_consts"])
                bank_wait(te, "pstat")
                for c in range(NCH):
                    te.matmul(pstat[0:1, :], onesc[:, :], colf(c),
                              start=(c == 0), stop=(c == NCH - 1))
                for c in range(NCH):
                    ins = te.matmul(pstat[32:33, :], onesc[:, :], sq[par][:, c * NT:(c + 1) * NT],
                                    start=(c == 0), stop=(c == NCH - 1))
                st["pe_stats"] = inc(ins, "pe")
            T(_)
            pe_stats = st["pe_stats"]

            RA, RB, RC = 0, 512, 1024
            def _(a):
                w(a, "pe", pe_stats)
                for k, v in rowfreeLN.items():
                    w(a, k, v)
                a.activation(rows[0:1, RA:RA + 512], pstat[0:1, :], AF.Copy, scale=1.0 / D)
                ins = a.activation(rows[0:1, RB:RB + 512], pstat[32:33, :], AF.Copy, scale=1.0 / D)
                cnt = inc(ins, "ac")
                a.wait_ge(SEM["ac"], cnt)
                ins = a.activation(rows[0:1, RC:RC + 512], rows[0:1, RA:RA + 512], AF.Square)
                st["ac_rows1"] = inc(ins, "ac")
            A(_)
            bank_read("pstat", "ac", st["ac_rows1"])

            def _(v):
                w(v, "ac", st["ac_rows1"])
                ins = v.tensor_tensor(rows[0:1, RB:RB + 512], rows[0:1, RB:RB + 512],
                                      rows[0:1, RC:RC + 512], ALU.subtract)
                st["dv_var"] = inc(ins, "dv")
            V(_)

            def _(a):
                w(a, "dv", st["dv_var"])
                ins = a.activation(rows[0:1, RC:RC + 512], rows[0:1, RB:RB + 512],
                                   AF.Sqrt, bias=epsc[0:1, 0:1])
                st["ac_sqv"] = inc(ins, "ac")
            A(_)

            def _(v):
                w(v, "ac", st["ac_sqv"])
                for j in range(4):
                    ins = v.reciprocal(rows[0:1, RB + j * 128:RB + (j + 1) * 128],
                                       rows[0:1, RC + j * 128:RC + (j + 1) * 128])
                cnt = inc(ins, "dv")
                v.wait_ge(SEM["dv"], cnt)
                ins = v.tensor_copy(rowsb[0:1, 0:512], rows[0:1, RB:RB + 512])
                cnt = inc(ins, "dv")
                ins = v.tensor_tensor(rows[0:1, RA:RA + 512], rows[0:1, RA:RA + 512],
                                      rows[0:1, RB:RB + 512], ALU.mult)
                cnt = inc(ins, "dv")
                v.wait_ge(SEM["dv"], cnt)
                ins = v.tensor_copy(rowsb[0:1, 512:1024], rows[0:1, RA:RA + 512])
                st["dv_rows2"] = inc(ins, "dv")
            V(_)

            def _(te):
                w(te, "dv", st["dv_rows2"])
                bank_wait(te, "pb1")
                bank_wait(te, "pb2")
                te.matmul(pb1[:, :], onesr[:, :], rowsb[0:1, 0:512],
                          start=True, stop=True)
                ins = te.matmul(pb2[:, :], onesr[:, :], rowsb[0:1, 512:1024],
                                start=True, stop=True)
                st["pe_bcast"] = inc(ins, "pe")
            T(_)
            rowfreeLN.clear()
            rowfreeLN["pe"] = st["pe_bcast"]

            def _(v):
                w(v, "pe", st["pe_bcast"])
                for k, v_ in dst_wait:
                    w(v, k, v_)
                for c in range(NCH):
                    v.tensor_tensor(dst_colf(c), colf(c), pb1[:, :], ALU.mult)
                    ins = v.tensor_tensor(dst_colf(c), dst_colf(c), pb2[:, :], ALU.subtract)
                st["dv_h"] = inc(ins, "dv")
            V(_)
            bank_read("pb1", "dv", st["dv_h"])
            bank_read("pb2", "dv", st["dv_h"])
            return st["dv_h"]

        def emit_proj(wsb, hsrc, dv_h, evac, extra_pe_wait=()):
            for c in range(NCH):
                bn = "pb4" if c % 2 == 0 else "pb5"
                bk = PB[bn]

                def _(te, c=c, bn=bn, bk=bk):
                    if c == 0:
                        w(te, "dv", dv_h)
                        for k, v in extra_pe_wait:
                            w(te, k, v)
                    bank_wait(te, bn)
                    for k in range(NCH):
                        ins = te.matmul(bk[:, :],
                                        wsb[:, k * 1024 + c * 128:k * 1024 + (c + 1) * 128],
                                        hsrc[:, k * NT:(k + 1) * NT],
                                        start=(k == 0), stop=(k == NCH - 1))
                    st["pe_proj"] = inc(ins, "pe")
                T(_)
                evac(c, bn, bk, st["pe_proj"])

        def emit_l2a(par, ac_sq_all, RD, rfree):
            def _(te):
                w(te, "ac", ac_sq_all)
                bank_wait(te, "pstat")
                for c in range(NCH):
                    ins = te.matmul(pstat[64:65, :], onesc[:, :], sq[par][:, c * NT:(c + 1) * NT],
                                    start=(c == 0), stop=(c == NCH - 1))
                st["pe_ksum"] = inc(ins, "pe")
            T(_)

            def _(a):
                w(a, "pe", st["pe_ksum"])
                for k, v in rfree.items():
                    w(a, k, v)
                ins = a.activation(rows[0:1, RD:RD + 512], pstat[64:65, :], AF.Copy)
                st["ac_ksum"] = inc(ins, "ac")
            A(_)
            bank_read("pstat", "ac", st["ac_ksum"])
            return st["pe_ksum"], st["ac_ksum"]

        def emit_l2b(par, ac_ksum, RD, RE, rfree, rb_off):
            def _(a):
                w(a, "ac", ac_ksum)
                ins = a.activation(rows[0:1, RE:RE + 512],
                                   rows[0:1, RD:RD + 512], AF.Sqrt)
                st["ac_nrm"] = inc(ins, "ac")
            A(_)

            def _(v):
                w(v, "ac", st["ac_nrm"])
                ins = v.tensor_scalar(rows[0:1, RE:RE + 512],
                                      rows[0:1, RE:RE + 512], EPS_L2, None, ALU.max)
                cnt = inc(ins, "dv")
                v.wait_ge(SEM["dv"], cnt)
                for j in range(4):
                    ins = v.reciprocal(rows[0:1, RD + j * 128:RD + (j + 1) * 128],
                                       rows[0:1, RE + j * 128:RE + (j + 1) * 128])
                cnt = inc(ins, "dv")
                v.wait_ge(SEM["dv"], cnt)
                ins = v.tensor_copy(rowsb[0:1, rb_off:rb_off + 512],
                                    rows[0:1, RD:RD + 512])
                st["dv_rsq"] = inc(ins, "dv")
            V(_)

            def _(te):
                w(te, "dv", st["dv_rsq"])
                bank_wait(te, "pb3")
                ins = te.matmul(pb3[:, :], onesr[:, :], rowsb[0:1, rb_off:rb_off + 512],
                                start=True, stop=True)
                st["pe_rsqb"] = inc(ins, "pe")
            T(_)
            rfree["pe"] = st["pe_rsqb"]

            def _(a):
                w(a, "pe", st["pe_rsqb"])
                w(a, "dv", st["dv_rsqbS_reads"])
                ins = a.activation(rsqbS[:, :], pb3[:, :], AF.Copy)
                st["ac_rsqbS"] = inc(ins, "ac")
            A(_)
            bank_read("pb3", "ac", st["ac_rsqbS"])
            return st["ac_rsqbS"]

        ph1 = {}

        def p1_A(i):
            par = i % 2
            skey = f"x{par}"

            def _(g, par=par, skey=skey, i=i):
                k, v = xfree[par]
                w(g, k, v)
                ins = g.dma_start(xb[par][:].rearrange("p (c n) -> p c n", c=NCH),
                                  hx[:, i * NT:(i + 1) * NT].rearrange("(c p) n -> p c n", p=128))
                inc(ins, skey)
            G(_)
            xcnt[par] += 1
            xw = [(skey, 16 * xcnt[par])]
            if i == 0:
                sqfree[0] = [("dv", st["dv_consts"])]
            xcol = (lambda par: lambda c: xb[par][:, c * NT:(c + 1) * NT])(par)
            hcol = (lambda par: lambda c: hb[par][:, c * NT:(c + 1) * NT])(par)
            dv_h = emit_ln(par, xcol, xw, hcol, [hfree[par]], flat_src=xb[par][:, :])
            xfree[par] = ("dv", dv_h)
            ph1[i] = dv_h

        def p1_B1(i):
            par = i % 2
            dv_h = ph1[i]
            ksq_cnt = {}
            krs_cnt = {}

            def kevac(c, bn, bk, pe_cnt, par=par):
                def _(a, c=c, bk=bk):
                    w(a, "pe", pe_cnt)
                    ins = a.activation(sq[par][:, c * NT:(c + 1) * NT], bk[:, :], AF.Square)
                    ksq_cnt[c] = inc(ins, "ac")
                A(_)

                def _(v, c=c, bk=bk):
                    w(v, "pe", pe_cnt)
                    ins = v.tensor_copy(qk[:, c * NT:(c + 1) * NT], bk[:, :])
                    krs_cnt[c] = inc(ins, "dv")
                V(_)
                bank_read(bn, "ac", ksq_cnt[c])
                bank_read(bn, "dv", krs_cnt[c])

            emit_proj(wk_sb, hb[par], dv_h, kevac,
                      extra_pe_wait=[("c", st["c_init"])])
            pe_k, ac_k = emit_l2a(par, ksq_cnt[NCH - 1], 1536, rowfreeL2)
            return ac_k

        def p1_B2(i, ac_k):
            par = i % 2
            dv_h = ph1[i]
            ac_rsqbS = emit_l2b(par, ac_k, 1536, 2048, rowfreeL2, 1024)

            vrs_cnt = {}
            red_cnt = {}

            def vevac(c, bn, bk, pe_cnt, par=par):
                def _(v, c=c, bk=bk):
                    w(v, "pe", pe_cnt)
                    w(v, "ac", ac_rsqbS)
                    v.tensor_tensor(sq[par][:, c * NT:(c + 1) * NT], bk[:, :],
                                    rsqbS[:, :], ALU.mult)
                    ins = v.tensor_tensor(qk[:, c * NT:(c + 1) * NT],
                                          qk[:, c * NT:(c + 1) * NT],
                                          sq[par][:, c * NT:(c + 1) * NT], ALU.mult)
                    vrs_cnt[c] = inc(ins, "dv")
                    ins = v.reduce_sum(gft[:, c:c + 1], qk[:, c * NT:(c + 1) * NT], AX.X)
                    red_cnt[c] = inc(ins, "dv")
                V(_)
                bank_read(bn, "dv", vrs_cnt[c])

            emit_proj(wv_sb, hb[par], dv_h, vevac)
            hfree[par] = ("pe", st["pe_proj"])
            st["dv_rsqbS_reads"] = vrs_cnt[NCH - 1]

            def _(v):
                v.wait_ge(SEM["dv"], red_cnt[NCH - 1])
                w(v, "gp", st["gp_consts"])
                ins = v.tensor_tensor(gfacc[:, :], gfacc[:, :], gft[:, :], ALU.add)
                st["dv_gf"] = inc(ins, "dv")
            V(_)
            sqfree[par] = [("dv", red_cnt[NCH - 1])]

        p1_A(0)
        p1_A(0)
        for i in range(8):
            ac_k = p1_B1(i)
            if i + 1 < 8:
                p1_A(i + 1)
            p1_B2(i, ac_k)

        pe_phase1 = C["pe"]
        xloaded = {}

        for t2 in range(2):
            par2 = t2 % 2

            def _(g, par2=par2, t2=t2):
                k, v = xfree[par2]
                w(g, k, v)
                ins = g.dma_start(xb[par2][:].rearrange("p (c n) -> p c n", c=NCH),
                                  hx[:, t2 * NT:(t2 + 1) * NT].rearrange("(c p) n -> p c n", p=128))
                inc(ins, f"x{par2}")
            G(_)
            xcnt[par2] += 1
            xloaded[(0, t2)] = 16 * xcnt[par2]

        def _(g):
            w(g, "dv", st["dv_gf"])
            ins = g.dma_start(hgin.ap(), gfacc[:, :])
            inc(ins, "c")
            w(g, "c", C["c"])
            ins = g.collective_compute("AllReduce", ALU.add,
                                       [[0, 1], [2, 3], [4, 5], [6, 7]],
                                       [hgin.ap().opt()], [hgout.ap().opt()])
            inc(ins, "gp")
            w(g, "gp", C["gp"])
            ins = g.dma_start(gfacc[:, :], hgout.ap())
            inc(ins, "c")
        G(_)

        gf_c_cnt = C["c"]

        def emit_gf_finalize():
            def _(a):
                w(a, "c", gf_c_cnt)
                ins = a.activation(gfsb[:, :], gfacc[:, :], AF.Copy, scale=1.0 / S)
                st["ac_gf"] = inc(ins, "ac")
            A(_)

        w12cnt = [0, 0]
        w12free = [("pe", pe_phase1), ("pe", 0)]
        hidfree = [("pe", 0), ("pe", 0)]
        xloaded = {}
        w3cnt = 0
        w3free = ("pe", 0)
        h2free = ("pe", 0)
        pr8free = ("pe", 0)
        unit_idx = 0

        for q in range(NQRT):
            qoff = q * NQ
            if q > 0:
                def _(g):
                    k, v = w3free
                    w(g, k, v)
                    ins = g.dma_start(w3wq[:].rearrange("p (k m) -> p k m", k=NCH),
                                      hwq.rearrange("(k p) m -> p k m", p=128))
                    inc(ins, "c")
                G(_)
            wq_wait = ("c", st["c_init"] if q == 0 else C["c"])

            at = [{}, {}]

            def s1(t):
                par = (q * 2 + t) % 2
                skey = f"x{par}"
                n0 = qoff + t * NT
                if (q, t) not in xloaded:
                    def _(g, par=par, skey=skey, n0=n0):
                        k, v = xfree[par]
                        w(g, k, v)
                        ins = g.dma_start(xb[par][:].rearrange("p (c n) -> p c n", c=NCH),
                                          hx[:, n0:n0 + NT].rearrange("(c p) n -> p c n", p=128))
                        inc(ins, skey)
                    G(_)
                    xcnt[par] += 1
                    xloaded[(q, t)] = 16 * xcnt[par]
                xw = [(skey, xloaded[(q, t)])]
                xcol = (lambda par: lambda c: xb[par][:, c * NT:(c + 1) * NT])(par)
                hcol = (lambda par: lambda c: hb[par][:, c * NT:(c + 1) * NT])(par)
                at[t]["dv_h"] = emit_ln(par, xcol, xw, hcol, [hfree[par]],
                                        flat_src=xb[par][:, :])

            def s2a(t):
                par = (q * 2 + t) % 2
                qsq_cnt = {}

                def qevac(c, bn, bk, pe_cnt, par=par, t=t):
                    def _(a, c=c, bk=bk):
                        w(a, "pe", pe_cnt)
                        ins = a.activation(sq[par][:, c * NT:(c + 1) * NT], bk[:, :], AF.Square)
                        qsq_cnt[c] = inc(ins, "ac")
                    A(_)

                    def _(v, c=c, bk=bk, t=t):
                        w(v, "pe", pe_cnt)
                        if c == 0 and t == 0:
                            w(v, "out", 16 * q)
                        ins = v.tensor_copy(
                            ysb[:, c * NQ + t * NT:c * NQ + (t + 1) * NT], bk[:, :])
                        st["dv_qrs"] = inc(ins, "dv")
                    V(_)
                    bank_read(bn, "ac", qsq_cnt[c])
                    bank_read(bn, "dv", st["dv_qrs"])

                emit_proj(w3wq, hb[par], at[t]["dv_h"], qevac, extra_pe_wait=[wq_wait])
                hfree[par] = ("pe", st["pe_proj"])
                st["pe_qproj"] = st["pe_proj"]
                RD = 1536 if t == 0 else 0
                rfree = rowfreeL2 if t == 0 else rowfreeLN
                pe_k, ac_k = emit_l2a(par, qsq_cnt[NCH - 1], RD, rfree)
                at[t]["ac_k"] = ac_k
                at[t]["pe_ksum"] = pe_k

            def s2b(t):
                par = (q * 2 + t) % 2
                RD = 1536 if t == 0 else 0
                RE = 2048 if t == 0 else 512
                rfree = rowfreeL2 if t == 0 else rowfreeLN
                rboff = 1024 if t == 0 else 0
                ac_rsqbS = emit_l2b(par, at[t]["ac_k"], RD, RE, rfree, rboff)

                def _(v, par=par, t=t, q=q, ac_rsqbS=ac_rsqbS):
                    w(v, "ac", ac_rsqbS)
                    w(v, "ac", st["ac_gf"])
                    for c in range(NCH):
                        ys = ysb[:, c * NQ + t * NT:c * NQ + (t + 1) * NT]
                        v.tensor_tensor(ys, ys, rsqbS[:, :], ALU.mult)
                        ins = v.scalar_tensor_tensor(
                            ys, ys, gfsb[:, c:c + 1],
                            xb[par][:, c * NT:(c + 1) * NT], ALU.mult, ALU.add)
                    st["dv_y"] = inc(ins, "dv")
                V(_)
                st["dv_rsqbS_reads"] = st["dv_y"]
                xfree[par] = ("dv", st["dv_y"])
                at[t]["dv_y"] = st["dv_y"]
                sqfree[par] = [("pe", at[t]["pe_ksum"])]

            def s3(t):
                par = (q * 2 + t) % 2
                ycol = (lambda t: lambda c: ysb[:, c * NQ + t * NT:c * NQ + (t + 1) * NT])(t)
                h2col = (lambda t: lambda c: h2sb[:, c * NQ + t * NT:c * NQ + (t + 1) * NT])(t)
                at[t]["dv_h2"] = emit_ln(par, ycol, [("dv", at[t]["dv_y"])], h2col, [h2free])
                sqfree[par] = [("pe", st["pe_stats"])]

            def s4(t):
                def _(te, t=t):
                    w(te, "dv", at[t]["dv_h2"])
                    bank_wait(te, "pb7")
                    for g in range(4):
                        for c in range(NCH):
                            ins = te.matmul(
                                pb7[:, t * 32 + g * 8:t * 32 + g * 8 + 8],
                                h2sb[:, c * NQ + t * NT + g * 128:c * NQ + t * NT + (g + 1) * 128],
                                gatesb[:, c * NE:(c + 1) * NE],
                                start=(c == 0), stop=(c == NCH - 1))
                    st["pe_scores"] = inc(ins, "pe")
                T(_)
                pe_scores = st["pe_scores"]

                for g in range(4):
                    def _(v, t=t, g=g):
                        w(v, "pe", pe_scores)
                        sc = pb7[:, t * 32 + g * 8:t * 32 + g * 8 + 8]
                        ins = v.reduce_max(colsc[:, 0:1], sc, AX.X)
                        cnt = inc(ins, "dv")
                        v.wait_ge(SEM["dv"], cnt)
                        ins = v.tensor_scalar(scr8[:, 0:8], sc, colsc[:, 0:1], None, ALU.is_equal)
                        cnt = inc(ins, "dv")
                        v.wait_ge(SEM["dv"], cnt)
                        ins = v.scalar_tensor_tensor(scr8[:, 8:16], scr8[:, 0:8], -1e30,
                                                     sc, ALU.mult, ALU.add)
                        cnt = inc(ins, "dv")
                        v.wait_ge(SEM["dv"], cnt)
                        ins = v.reduce_max(colsc[:, 1:2], scr8[:, 8:16], AX.X)
                        cnt = inc(ins, "dv")
                        v.wait_ge(SEM["dv"], cnt)
                        ins = v.tensor_tensor(colsc[:, 2:3], colsc[:, 0:1], colsc[:, 1:2],
                                              ALU.subtract)
                        st["dv_d"] = inc(ins, "dv")
                    V(_)

                    def _(a):
                        w(a, "dv", st["dv_d"])
                        ins = a.activation(colsc[:, 3:4], colsc[:, 2:3], AF.Sigmoid)
                        st["ac_p1"] = inc(ins, "ac")
                    A(_)

                    def _(v, t=t, g=g):
                        w(v, "ac", st["ac_p1"])
                        ins = v.tensor_scalar(colsc[:, 4:5], colsc[:, 3:4], -1.0, 1.0,
                                              ALU.mult, ALU.add)
                        cnt = inc(ins, "dv")
                        v.wait_ge(SEM["dv"], cnt)
                        ins = v.tensor_scalar(scr8[:, 16:24], scr8[:, 8:16], colsc[:, 1:2],
                                              None, ALU.is_equal)
                        cnt = inc(ins, "dv")
                        v.wait_ge(SEM["dv"], cnt)
                        ins = v.tensor_scalar(scr8[:, 24:32], scr8[:, 0:8], colsc[:, 3:4],
                                              None, ALU.mult)
                        cnt = inc(ins, "dv")
                        v.wait_ge(SEM["dv"], cnt)
                        ins = v.tensor_scalar(scr8[:, 16:24], scr8[:, 16:24], colsc[:, 4:5],
                                              None, ALU.mult)
                        cnt = inc(ins, "dv")
                        v.wait_ge(SEM["dv"], cnt)
                        ins = v.tensor_tensor(probs[:, t * 32 + g * 8:t * 32 + g * 8 + 8],
                                              scr8[:, 24:32], scr8[:, 16:24], ALU.add)
                        st["dv_probs"] = inc(ins, "dv")
                    V(_)
                    bank_read("pb7", "dv", st["dv_probs"])

                    def _(te, t=t, g=g):
                        bank_wait(te, "pb7")
                        w(te, "dv", st["dv_consts"])
                        k, v = pr8free
                        w(te, k, v)
                        ins = te.transpose(ptr, probs[:, t * 32 + g * 8:t * 32 + g * 8 + 8],
                                           ident[:, :])
                        st["pe_tr"] = inc(ins, "pe")
                    T(_)

                    def _(v, t=t, g=g):
                        w(v, "pe", st["pe_tr"])
                        ins = v.tensor_copy(pr8[0:8, t * NT + g * 128:t * NT + (g + 1) * 128], ptr)
                        st["dv_pr8"] = inc(ins, "dv")
                    V(_)
                    bank_read("pb7", "dv", st["dv_pr8"])

            s1(0); s1(1); s2a(0); s2a(1)
            if q == 0:
                emit_gf_finalize()
            s2b(0); s2b(1)
            s3(0); s3(1); s4(0); s4(1)
            st["dv_h2_1"] = at[1]["dv_h2"]
            st["pe_proj"] = st["pe_qproj"]

            dv_h2_all = st["dv_h2_1"]
            dv_pr8_all = st["dv_pr8"]
            w3free = ("pe", st["pe_proj"])

            if q + 1 < NQRT:
                for t2 in range(2):
                    par2 = ((q + 1) * 2 + t2) % 2
                    skey2 = f"x{par2}"
                    n02 = (q + 1) * NQ + t2 * NT

                    def _(g, par2=par2, skey2=skey2, n02=n02):
                        k, v = xfree[par2]
                        w(g, k, v)
                        ins = g.dma_start(xb[par2][:].rearrange("p (c n) -> p c n", c=NCH),
                                          hx[:, n02:n02 + NT].rearrange("(c p) n -> p c n", p=128))
                        inc(ins, skey2)
                    G(_)
                    xcnt[par2] += 1
                    xloaded[(q + 1, t2)] = 16 * xcnt[par2]

            for e in range(NE):
                for t in range(2):
                    def _(te, e=e, t=t):
                        w(te, "dv", dv_pr8_all)
                        bank_wait(te, "pb3")
                        ins = te.matmul(pb3[:, :], onehot[:, e * 128:(e + 1) * 128],
                                        pr8[0:8, t * NT:(t + 1) * NT], start=True, stop=True)
                        st["pe_pb"] = inc(ins, "pe")
                    T(_)

                    def _(v, t=t):
                        w(v, "pe", st["pe_pb"])
                        ins = v.tensor_copy(pbS[:, t * NT:(t + 1) * NT], pb3[:, :])
                        st["dv_pb"] = inc(ins, "dv")
                    V(_)
                    bank_read("pb3", "dv", st["dv_pb"])
                dv_pb = st["dv_pb"]
                pe_pb_last = st["pe_pb"]

                for hh in range(2):
                    slot = unit_idx % 2
                    wkey = f"w{slot}"

                    def _(g, e=e, hh=hh, slot=slot, wkey=wkey):
                        k, v = w12free[slot]
                        w(g, k, v)
                        ins = g.dma_start(w12sb[slot].rearrange("p (k h) -> p k h", k=NCH),
                                          hw12[e, hh].rearrange("(k p) h -> p k h", p=128))
                        inc(ins, wkey)
                        k, v = w3free
                        w(g, k, v)
                        ins = g.dma_start(w3wq[:].rearrange("p (j m) -> p j m", j=NCH),
                                          hw3[e, hh].rearrange("(j p) m -> p j m", p=128))
                        inc(ins, "w3")
                    G(_)
                    w12cnt[slot] += 1
                    w3cnt += 1

                    hid_dv_t = [{}, {}]
                    for t in range(2):
                        hid_dv = hid_dv_t[t]
                        for j in range(8):
                            bU1n, bU2n = ("pb1", "pb2") if j % 2 == 0 else ("pb3", "pb4")
                            bU1, bU2 = PB[bU1n], PB[bU2n]

                            def _(te, t=t, j=j, slot=slot, wkey=wkey,
                                  bU1n=bU1n, bU2n=bU2n, bU1=bU1, bU2=bU2):
                                if j == 0 and t == 0:
                                    w(te, wkey, 16 * w12cnt[slot])
                                    w(te, "dv", max(dv_h2_all, dv_pb))
                                bank_wait(te, bU1n)
                                for k in range(NCH):
                                    ins = te.matmul(
                                        bU1[:, :],
                                        w12sb[slot][:, k * 2048 + j * 128:k * 2048 + (j + 1) * 128],
                                        h2sb[:, k * NQ + t * NT:k * NQ + (t + 1) * NT],
                                        start=(k == 0), stop=(k == NCH - 1))
                                cnt1 = inc(ins, "pe")
                                bank_wait(te, bU2n)
                                for k in range(NCH):
                                    ins = te.matmul(
                                        bU2[:, :],
                                        w12sb[slot][:, k * 2048 + (8 + j) * 128:k * 2048 + (9 + j) * 128],
                                        h2sb[:, k * NQ + t * NT:k * NQ + (t + 1) * NT],
                                        start=(k == 0), stop=(k == NCH - 1))
                                st["pe_u"] = (cnt1, inc(ins, "pe"))
                            T(_)
                            pe_u1, pe_u2 = st["pe_u"]

                            def _(a, j=j, bU1=bU1, pe_u1=pe_u1):
                                w(a, "pe", pe_u1)
                                w(a, "dv", hid_dv.get(j - 1, 0))
                                ins = a.activation(tmps[j % 2][:, :], bU1[:, :], AF.Silu)
                                st["ac_silu"] = inc(ins, "ac")
                            A(_)
                            bank_read(bU1n, "ac", st["ac_silu"])

                            def _(v, j=j, t=t, bU2=bU2, pe_u2=pe_u2):
                                w(v, "pe", pe_u2)
                                w(v, "ac", st["ac_silu"])
                                k_, v_ = hidfree[t]
                                w(v, k_, v_)
                                ins = v.tensor_tensor(hid[t][:, j * NT:(j + 1) * NT],
                                                      tmps[j % 2][:, :], bU2[:, :], ALU.mult)
                                hid_dv[j] = inc(ins, "dv")
                            V(_)
                            bank_read(bU2n, "dv", hid_dv[j])

                    for t in range(2):
                        dv_hid = hid_dv_t[t][7]
                        for m in range(NCH):
                            bEn = "pb5" if m % 2 == 0 else "pb6"
                            bE = PB[bEn]

                            def _(te, m=m, t=t, bEn=bEn, bE=bE, dv_hid=dv_hid):
                                if m == 0:
                                    w(te, "dv", dv_hid)
                                    w(te, "w3", 16 * w3cnt)
                                bank_wait(te, bEn)
                                for j in range(NCH):
                                    ins = te.matmul(
                                        bE[:, :],
                                        w3wq[:, j * 1024 + m * 128:j * 1024 + (m + 1) * 128],
                                        hid[t][:, j * NT:(j + 1) * NT],
                                        start=(j == 0), stop=(j == NCH - 1))
                                st["pe_eo"] = inc(ins, "pe")
                                if m == NCH - 1:
                                    hidfree[t] = ("pe", st["pe_eo"])
                            T(_)

                            def _(v, m=m, t=t, bE=bE):
                                w(v, "pe", st["pe_eo"])
                                v.tensor_tensor(tmpe[:, :], bE[:, :],
                                                pbS[:, t * NT:(t + 1) * NT], ALU.mult)
                                ins = v.tensor_tensor(
                                    ysb[:, m * NQ + t * NT:m * NQ + (t + 1) * NT],
                                    ysb[:, m * NQ + t * NT:m * NQ + (t + 1) * NT],
                                    tmpe[:, :], ALU.add)
                                st["dv_yadd"] = inc(ins, "dv")
                            V(_)
                            bank_read(bEn, "dv", st["dv_yadd"])

                    w12free[slot] = ("pe", st["pe_u"][1])
                    w3free = ("pe", st["pe_eo"])
                    unit_idx += 1

            h2free = ("pe", st["pe_u"][1])
            pr8free = ("pe", pe_pb_last)

            def _(g, q=q):
                w(g, "dv", st["dv_yadd"])
                ins = g.dma_start(
                    hout.rearrange("(m p) n -> p m n", p=128)[:, :, q * NQ:(q + 1) * NQ],
                    ysb[:].rearrange("p (m n) -> p m n", m=NCH))
                inc(ins, "out")
            G(_)

        def _(g):
            g.wait_ge(SEM["out"], C["out"])
        G(_)

        def _(sy):
            sy.wait_ge(SEM["out"], C["out"])
        block.sync(_)

    return nc


_NC_CACHE = {}


def kernel(L0_attn_x, L0_attn_W_q, L0_attn_W_k, L0_attn_W_v,
           L0_ffn_w12, L0_ffn_w3, L0_ffn_gate_w):
    x = np.asarray(L0_attn_x, dtype=np.float32)
    Wq = np.asarray(L0_attn_W_q, dtype=np.float32)
    Wk = np.asarray(L0_attn_W_k, dtype=np.float32)
    Wv = np.asarray(L0_attn_W_v, dtype=np.float32)
    w12 = np.asarray(L0_ffn_w12, dtype=np.float32)
    w3 = np.asarray(L0_ffn_w3, dtype=np.float32)
    gate = np.asarray(L0_ffn_gate_w, dtype=np.float32)

    B, Sq, Dm = x.shape
    xT = np.ascontiguousarray(x.transpose(0, 2, 1))
    WqT = np.ascontiguousarray(Wq.T)
    WkT = np.ascontiguousarray(Wk.T)
    WvT = np.ascontiguousarray(Wv.T)
    gateT = np.ascontiguousarray(gate.T)
    w12h = np.empty((NE, 2, D, 2048), np.float32)
    for e in range(NE):
        for hh in range(2):
            rws = np.concatenate([w12[e, hh * 1024:(hh + 1) * 1024],
                                  w12[e, 2048 + hh * 1024:2048 + (hh + 1) * 1024]], axis=0)
            w12h[e, hh] = rws.T
    w3h = np.empty((NE, 2, D, D), np.float32)
    for e in range(NE):
        for hh in range(2):
            w3h[e, hh] = w3[e][:, hh * 1024:(hh + 1) * 1024].T

    in_maps = []
    for c in range(8):
        b, half = c // 2, c % 2
        own = xT[b][:, half * OWN:(half + 1) * OWN]
        sib = xT[b][:, (1 - half) * OWN:(2 - half) * OWN]
        xc = np.ascontiguousarray(np.concatenate([own, sib], axis=1))
        in_maps.append({"x": xc, "wq": WqT, "wk": WkT, "wv": WvT, "gate": gateT,
                        "w12": w12h, "w3": w3h})

    if "nc" not in _NC_CACHE:
        _NC_CACHE["nc"] = build_nc()
    nc = _NC_CACHE["nc"]
    res = run_bass_kernel_spmd(nc, in_maps, core_ids=list(range(8)))

    out = np.empty((B, Sq, Dm), np.float32)
    for c in range(8):
        b, half = c // 2, c % 2
        out[b, half * OWN:(half + 1) * OWN] = res.results[c]["out"].T
    return out


# version 11
# speedup vs baseline: 1.0030x; 1.0030x over previous
import numpy as np
from contextlib import ExitStack

import concourse.bass as bass
import concourse.mybir as mybir
from concourse.bass_utils import run_bass_kernel_spmd

F32 = mybir.dt.float32
BF16 = mybir.dt.bfloat16
AF = mybir.ActivationFunctionType
ALU = mybir.AluOpType
AX = mybir.AxisListType

D = 1024
NCH = 8
S = 8192
OWN = 4096
NT = 512
NQ = 1024
NQRT = 4
NE = 8
EPS_LN = 1e-5
EPS_L2 = 1e-12


def build_nc():
    nc = bass.Bass("TRN2", target_bir_lowering=False, debug=False)
    hx = nc.dram_tensor("x", [8, 128, NCH * NT], F32, kind="ExternalInput").ap()
    hwq = nc.dram_tensor("wq", [128, NCH * D], F32, kind="ExternalInput").ap()
    hwk = nc.dram_tensor("wk", [128, NCH * D], F32, kind="ExternalInput").ap()
    hwv = nc.dram_tensor("wv", [128, NCH * D], F32, kind="ExternalInput").ap()
    hgate = nc.dram_tensor("gate", [128, NCH * NE], F32, kind="ExternalInput").ap()
    hw12 = nc.dram_tensor("w12", [NE, 2, 128, NCH * 2048], F32, kind="ExternalInput").ap()
    hw3 = nc.dram_tensor("w3", [NE, 2, 128, NCH * 1024], F32, kind="ExternalInput").ap()
    hout = nc.dram_tensor("out", [D, OWN], F32, kind="ExternalOutput").ap()
    hgin = nc.dram_tensor("gfin", [128, NCH], F32, kind="Internal")
    hgout = nc.dram_tensor("gfout", [128, NCH], F32, kind="Internal")

    with ExitStack() as ctx:
        en = ctx.enter_context
        xb = [en(nc.sbuf_tensor(f"xb{i}", [128, NCH * NT], BF16)) for i in range(2)]
        hb = [en(nc.sbuf_tensor(f"hb{i}", [128, NCH * NT], BF16)) for i in range(2)]
        sq = [en(nc.sbuf_tensor(f"sq{i}", [128, NCH * NT], BF16)) for i in range(2)]
        qk = en(nc.sbuf_tensor("qk", [128, NCH * NT], BF16))
        rsqbS = en(nc.sbuf_tensor("rsqbS", [128, NT], BF16))
        rows = en(nc.sbuf_tensor("rows", [1, 2560], F32))
        rowsb = en(nc.sbuf_tensor("rowsb", [1, 1536], BF16))
        onesc = en(nc.sbuf_tensor("onesc", [128, 1], BF16))
        onesr = en(nc.sbuf_tensor("onesr", [1, 128], BF16))
        ident = en(nc.sbuf_tensor("ident", [128, 128], BF16))
        pidx = en(nc.sbuf_tensor("pidx", [128, 1], F32))
        krep = en(nc.sbuf_tensor("krep", [8, 128], F32))
        onehot = en(nc.sbuf_tensor("onehot", [8, NE * 128], BF16))
        gfacc = en(nc.sbuf_tensor("gfacc", [128, NCH], F32))
        gft = en(nc.sbuf_tensor("gft", [128, NCH], F32))
        gfsb = en(nc.sbuf_tensor("gfsb", [128, NCH], F32))
        gatesb = en(nc.sbuf_tensor("gatesb", [128, NCH * NE], BF16))
        probs = en(nc.sbuf_tensor("probs", [128, 64], BF16))
        pr8 = en(nc.sbuf_tensor("pr8", [8, NQ], BF16))
        pbS = en(nc.sbuf_tensor("pbS", [128, NQ], BF16))
        colsc = en(nc.sbuf_tensor("colsc", [128, 16], F32))
        scr8 = en(nc.sbuf_tensor("scr8", [128, 4 * NE], F32))
        tmps = [en(nc.sbuf_tensor("tmps0", [128, NT], BF16))] * 2
        tmpe = en(nc.sbuf_tensor("tmpe", [128, NT], BF16))
        epsc = en(nc.sbuf_tensor("epsc", [128, 1], F32))
        wregion = en(nc.sbuf_tensor("wregion", [128, 16384], BF16))
        wk_sb = wregion[:, 0:8192]
        wv_sb = wregion[:, 8192:16384]
        w12sb = [wregion[:, :], en(nc.sbuf_tensor("w12B", [128, 16384], BF16))[:, :]]
        w3wq = en(nc.sbuf_tensor("w3wq", [128, 8192], BF16))
        ysb = en(nc.sbuf_tensor("ysb", [128, NCH * NQ], BF16))
        h2sb = en(nc.sbuf_tensor("h2sb", [128, NCH * NQ], BF16))
        hid = [en(nc.sbuf_tensor(f"hid{i}", [128, 8 * NT], BF16)) for i in range(2)]

        pstat = en(nc.psum_tensor("pstat", [65, 512], F32))
        pbt = [en(nc.psum_tensor(f"pb{i}", [128, 512], F32)) for i in range(1, 8)]
        pb1, pb2, pb3, pb4, pb5, pb6, pb7 = pbt
        PB = {"pb1": pb1, "pb2": pb2, "pb3": pb3, "pb4": pb4, "pb5": pb5,
              "pb6": pb6, "pb7": pb7}
        ptr = pb7[0:8, 64:128].bitcast(BF16)

        sem_names = ["pe", "dv", "ac", "gp", "x0", "x1", "w0", "w1", "w3", "c", "out"]
        SEM = {k: en(nc.semaphore(f"s_{k}")) for k in sem_names}
        C = {k: 0 for k in sem_names}
        DMAKEYS = ("x0", "x1", "w0", "w1", "w3", "c", "out")

        block = en(nc.Block())

        def inc(ins, key):
            n = 16 if key in DMAKEYS else 1
            ins.then_inc(SEM[key], n)
            C[key] += n
            return C[key]

        def w(eng, key, val):
            assert val <= C[key], (key, val, C[key])
            if val > 0:
                eng.wait_ge(SEM[key], val)

        bankf = {}

        def bank_wait(te, bank):
            for k, v in bankf.get(bank, {}).items():
                w(te, k, v)
            bankf[bank] = {}

        def bank_read(bank, key, cnt):
            d = bankf.setdefault(bank, {})
            d[key] = max(d.get(key, 0), cnt)

        T, V, A, G = block.tensor, block.vector, block.scalar, block.gpsimd
        st = {"dv_rsqbS_reads": 0}

        def _(g):
            g.memset(onesc[:, :], 1.0)
            g.memset(onesr[:, :], 1.0)
            g.memset(gfacc[:, :], 0.0)
            g.memset(epsc[:, :], EPS_LN)
            g.iota(pidx[:, :], [[0, 1]], channel_multiplier=1,
                   allow_small_or_imprecise_dtypes=True)
            g.iota(krep[:, :], [[0, 128]], channel_multiplier=1,
                   allow_small_or_imprecise_dtypes=True)
            cidx = sq[0][:, 0:256].bitcast(F32)
            ins = g.iota(cidx, [[1, 128]], channel_multiplier=0,
                         allow_small_or_imprecise_dtypes=True)
            st["gp_consts"] = inc(ins, "gp")
        G(_)

        def _(v):
            w(v, "gp", st["gp_consts"])
            cidx = sq[0][:, 0:256].bitcast(F32)
            ins = v.tensor_scalar(ident[:, :], cidx, pidx[:, :], None, ALU.is_equal)
            for e in range(NE):
                ins = v.tensor_scalar(onehot[:, e * 128:(e + 1) * 128], krep[:, :],
                                      float(e), None, ALU.is_equal)
            st["dv_consts"] = inc(ins, "dv")
        V(_)

        def _(g):
            ins = g.dma_start(wk_sb, hwk)
            inc(ins, "c")
            ins = g.dma_start(wv_sb, hwv)
            inc(ins, "c")
            ins = g.dma_start(gatesb[:, :], hgate)
            inc(ins, "c")
            ins = g.dma_start(w3wq[:, :], hwq)
            st["c_init"] = inc(ins, "c")
        G(_)

        xfree = [("pe", 0), ("pe", 0)]
        hfree = [("pe", 0), ("pe", 0)]
        sqfree = [[("dv", 0)], [("dv", 0)]]
        rowfreeLN = {}
        rowfreeL2 = {}
        xcnt = [0, 0]

        def emit_ln(par, colf, src_wait, dst_colf, dst_wait, flat_src=None):


            def _(a):
                for k, v in src_wait:
                    w(a, k, v)
                for k, v in sqfree[par]:
                    w(a, k, v)
                if flat_src is not None:
                    ins = a.activation(sq[par][:, :], flat_src, AF.Square)
                else:
                    for c in range(NCH):
                        ins = a.activation(sq[par][:, c * NT:(c + 1) * NT], colf(c), AF.Square)
                st["ac_sq"] = inc(ins, "ac")
            A(_)
            ac_sq = st["ac_sq"]

            def _(te):
                for k, v in src_wait:
                    w(te, k, v)
                w(te, "ac", ac_sq)
                w(te, "gp", st["gp_consts"])
                bank_wait(te, "pstat")
                for c in range(NCH):
                    te.matmul(pstat[0:1, :], onesc[:, :], colf(c),
                              start=(c == 0), stop=(c == NCH - 1))
                for c in range(NCH):
                    ins = te.matmul(pstat[32:33, :], onesc[:, :], sq[par][:, c * NT:(c + 1) * NT],
                                    start=(c == 0), stop=(c == NCH - 1))
                st["pe_stats"] = inc(ins, "pe")
            T(_)
            pe_stats = st["pe_stats"]

            RA, RB, RC = 0, 512, 1024
            def _(a):
                w(a, "pe", pe_stats)
                for k, v in rowfreeLN.items():
                    w(a, k, v)
                a.activation(rows[0:1, RA:RA + 512], pstat[0:1, :], AF.Copy, scale=1.0 / D)
                ins = a.activation(rows[0:1, RB:RB + 512], pstat[32:33, :], AF.Copy, scale=1.0 / D)
                cnt = inc(ins, "ac")
                a.wait_ge(SEM["ac"], cnt)
                ins = a.activation(rows[0:1, RC:RC + 512], rows[0:1, RA:RA + 512], AF.Square)
                st["ac_rows1"] = inc(ins, "ac")
            A(_)
            bank_read("pstat", "ac", st["ac_rows1"])

            def _(v):
                w(v, "ac", st["ac_rows1"])
                ins = v.tensor_tensor(rows[0:1, RB:RB + 512], rows[0:1, RB:RB + 512],
                                      rows[0:1, RC:RC + 512], ALU.subtract)
                st["dv_var"] = inc(ins, "dv")
            V(_)

            def _(a):
                w(a, "dv", st["dv_var"])
                ins = a.activation(rows[0:1, RC:RC + 512], rows[0:1, RB:RB + 512],
                                   AF.Sqrt, bias=epsc[0:1, 0:1])
                st["ac_sqv"] = inc(ins, "ac")
            A(_)

            def _(v):
                w(v, "ac", st["ac_sqv"])
                for j in range(4):
                    ins = v.reciprocal(rows[0:1, RB + j * 128:RB + (j + 1) * 128],
                                       rows[0:1, RC + j * 128:RC + (j + 1) * 128])
                cnt = inc(ins, "dv")
                v.wait_ge(SEM["dv"], cnt)
                ins = v.tensor_copy(rowsb[0:1, 0:512], rows[0:1, RB:RB + 512])
                cnt = inc(ins, "dv")
                ins = v.tensor_tensor(rows[0:1, RA:RA + 512], rows[0:1, RA:RA + 512],
                                      rows[0:1, RB:RB + 512], ALU.mult)
                cnt = inc(ins, "dv")
                v.wait_ge(SEM["dv"], cnt)
                ins = v.tensor_copy(rowsb[0:1, 512:1024], rows[0:1, RA:RA + 512])
                st["dv_rows2"] = inc(ins, "dv")
            V(_)

            def _(te):
                w(te, "dv", st["dv_rows2"])
                bank_wait(te, "pb1")
                bank_wait(te, "pb2")
                te.matmul(pb1[:, :], onesr[:, :], rowsb[0:1, 0:512],
                          start=True, stop=True)
                ins = te.matmul(pb2[:, :], onesr[:, :], rowsb[0:1, 512:1024],
                                start=True, stop=True)
                st["pe_bcast"] = inc(ins, "pe")
            T(_)
            rowfreeLN.clear()
            rowfreeLN["pe"] = st["pe_bcast"]

            def _(v):
                w(v, "pe", st["pe_bcast"])
                for k, v_ in dst_wait:
                    w(v, k, v_)
                for c in range(NCH):
                    v.tensor_tensor(dst_colf(c), colf(c), pb1[:, :], ALU.mult)
                    ins = v.tensor_tensor(dst_colf(c), dst_colf(c), pb2[:, :], ALU.subtract)
                st["dv_h"] = inc(ins, "dv")
            V(_)
            bank_read("pb1", "dv", st["dv_h"])
            bank_read("pb2", "dv", st["dv_h"])
            return st["dv_h"]

        def emit_proj(wsb, hsrc, dv_h, evac, extra_pe_wait=()):
            for c in range(NCH):
                bn = "pb4" if c % 2 == 0 else "pb5"
                bk = PB[bn]

                def _(te, c=c, bn=bn, bk=bk):
                    if c == 0:
                        w(te, "dv", dv_h)
                        for k, v in extra_pe_wait:
                            w(te, k, v)
                    bank_wait(te, bn)
                    for k in range(NCH):
                        ins = te.matmul(bk[:, :],
                                        wsb[:, k * 1024 + c * 128:k * 1024 + (c + 1) * 128],
                                        hsrc[:, k * NT:(k + 1) * NT],
                                        start=(k == 0), stop=(k == NCH - 1))
                    st["pe_proj"] = inc(ins, "pe")
                T(_)
                evac(c, bn, bk, st["pe_proj"])

        def emit_l2a(par, ac_sq_all, RD, rfree):
            def _(te):
                w(te, "ac", ac_sq_all)
                bank_wait(te, "pstat")
                for c in range(NCH):
                    ins = te.matmul(pstat[64:65, :], onesc[:, :], sq[par][:, c * NT:(c + 1) * NT],
                                    start=(c == 0), stop=(c == NCH - 1))
                st["pe_ksum"] = inc(ins, "pe")
            T(_)

            def _(a):
                w(a, "pe", st["pe_ksum"])
                for k, v in rfree.items():
                    w(a, k, v)
                ins = a.activation(rows[0:1, RD:RD + 512], pstat[64:65, :], AF.Copy)
                st["ac_ksum"] = inc(ins, "ac")
            A(_)
            bank_read("pstat", "ac", st["ac_ksum"])
            return st["pe_ksum"], st["ac_ksum"]

        def emit_l2b(par, ac_ksum, RD, RE, rfree, rb_off):
            def _(a):
                w(a, "ac", ac_ksum)
                ins = a.activation(rows[0:1, RE:RE + 512],
                                   rows[0:1, RD:RD + 512], AF.Sqrt)
                st["ac_nrm"] = inc(ins, "ac")
            A(_)

            def _(v):
                w(v, "ac", st["ac_nrm"])
                ins = v.tensor_scalar(rows[0:1, RE:RE + 512],
                                      rows[0:1, RE:RE + 512], EPS_L2, None, ALU.max)
                cnt = inc(ins, "dv")
                v.wait_ge(SEM["dv"], cnt)
                for j in range(4):
                    ins = v.reciprocal(rows[0:1, RD + j * 128:RD + (j + 1) * 128],
                                       rows[0:1, RE + j * 128:RE + (j + 1) * 128])
                cnt = inc(ins, "dv")
                v.wait_ge(SEM["dv"], cnt)
                ins = v.tensor_copy(rowsb[0:1, rb_off:rb_off + 512],
                                    rows[0:1, RD:RD + 512])
                st["dv_rsq"] = inc(ins, "dv")
            V(_)

            def _(te):
                w(te, "dv", st["dv_rsq"])
                bank_wait(te, "pb3")
                ins = te.matmul(pb3[:, :], onesr[:, :], rowsb[0:1, rb_off:rb_off + 512],
                                start=True, stop=True)
                st["pe_rsqb"] = inc(ins, "pe")
            T(_)
            rfree["pe"] = st["pe_rsqb"]

            def _(a):
                w(a, "pe", st["pe_rsqb"])
                w(a, "dv", st["dv_rsqbS_reads"])
                ins = a.activation(rsqbS[:, :], pb3[:, :], AF.Copy)
                st["ac_rsqbS"] = inc(ins, "ac")
            A(_)
            bank_read("pb3", "ac", st["ac_rsqbS"])
            return st["ac_rsqbS"]

        ph1 = {}

        def p1_A(i):
            par = i % 2
            skey = f"x{par}"

            def _(g, par=par, skey=skey, i=i):
                k, v = xfree[par]
                w(g, k, v)
                ins = g.dma_start(xb[par][:, :], hx[i])
                inc(ins, skey)
            G(_)
            xcnt[par] += 1
            xw = [(skey, 16 * xcnt[par])]
            if i == 0:
                sqfree[0] = [("dv", st["dv_consts"])]
            xcol = (lambda par: lambda c: xb[par][:, c * NT:(c + 1) * NT])(par)
            hcol = (lambda par: lambda c: hb[par][:, c * NT:(c + 1) * NT])(par)
            dv_h = emit_ln(par, xcol, xw, hcol, [hfree[par]], flat_src=xb[par][:, :])
            xfree[par] = ("dv", dv_h)
            ph1[i] = dv_h

        def p1_B1(i):
            par = i % 2
            dv_h = ph1[i]
            ksq_cnt = {}
            krs_cnt = {}

            def kevac(c, bn, bk, pe_cnt, par=par):
                def _(a, c=c, bk=bk):
                    w(a, "pe", pe_cnt)
                    ins = a.activation(sq[par][:, c * NT:(c + 1) * NT], bk[:, :], AF.Square)
                    ksq_cnt[c] = inc(ins, "ac")
                A(_)

                def _(v, c=c, bk=bk):
                    w(v, "pe", pe_cnt)
                    ins = v.tensor_copy(qk[:, c * NT:(c + 1) * NT], bk[:, :])
                    krs_cnt[c] = inc(ins, "dv")
                V(_)
                bank_read(bn, "ac", ksq_cnt[c])
                bank_read(bn, "dv", krs_cnt[c])

            emit_proj(wk_sb, hb[par], dv_h, kevac,
                      extra_pe_wait=[("c", st["c_init"])])
            pe_k, ac_k = emit_l2a(par, ksq_cnt[NCH - 1], 1536, rowfreeL2)
            return ac_k

        def p1_B2(i, ac_k):
            par = i % 2
            dv_h = ph1[i]
            ac_rsqbS = emit_l2b(par, ac_k, 1536, 2048, rowfreeL2, 1024)

            vrs_cnt = {}
            red_cnt = {}

            def vevac(c, bn, bk, pe_cnt, par=par):
                def _(v, c=c, bk=bk):
                    w(v, "pe", pe_cnt)
                    w(v, "ac", ac_rsqbS)
                    v.tensor_tensor(sq[par][:, c * NT:(c + 1) * NT], bk[:, :],
                                    rsqbS[:, :], ALU.mult)
                    ins = v.tensor_tensor(qk[:, c * NT:(c + 1) * NT],
                                          qk[:, c * NT:(c + 1) * NT],
                                          sq[par][:, c * NT:(c + 1) * NT], ALU.mult)
                    vrs_cnt[c] = inc(ins, "dv")
                    ins = v.reduce_sum(gft[:, c:c + 1], qk[:, c * NT:(c + 1) * NT], AX.X)
                    red_cnt[c] = inc(ins, "dv")
                V(_)
                bank_read(bn, "dv", vrs_cnt[c])

            emit_proj(wv_sb, hb[par], dv_h, vevac)
            hfree[par] = ("pe", st["pe_proj"])
            st["dv_rsqbS_reads"] = vrs_cnt[NCH - 1]

            def _(v):
                v.wait_ge(SEM["dv"], red_cnt[NCH - 1])
                w(v, "gp", st["gp_consts"])
                ins = v.tensor_tensor(gfacc[:, :], gfacc[:, :], gft[:, :], ALU.add)
                st["dv_gf"] = inc(ins, "dv")
            V(_)
            sqfree[par] = [("dv", red_cnt[NCH - 1])]

        p1_A(0)
        p1_A(0)
        for i in range(8):
            ac_k = p1_B1(i)
            if i + 1 < 8:
                p1_A(i + 1)
            p1_B2(i, ac_k)

        pe_phase1 = C["pe"]
        xloaded = {}

        for t2 in range(2):
            par2 = t2 % 2

            def _(g, par2=par2, t2=t2):
                k, v = xfree[par2]
                w(g, k, v)
                ins = g.dma_start(xb[par2][:, :], hx[t2])
                inc(ins, f"x{par2}")
            G(_)
            xcnt[par2] += 1
            xloaded[(0, t2)] = 16 * xcnt[par2]

        def _(g):
            w(g, "dv", st["dv_gf"])
            ins = g.dma_start(hgin.ap(), gfacc[:, :])
            inc(ins, "c")
            w(g, "c", C["c"])
            ins = g.collective_compute("AllReduce", ALU.add,
                                       [[0, 1], [2, 3], [4, 5], [6, 7]],
                                       [hgin.ap().opt()], [hgout.ap().opt()])
            inc(ins, "gp")
            w(g, "gp", C["gp"])
            ins = g.dma_start(gfacc[:, :], hgout.ap())
            inc(ins, "c")
        G(_)

        gf_c_cnt = C["c"]

        def emit_gf_finalize():
            def _(a):
                w(a, "c", gf_c_cnt)
                ins = a.activation(gfsb[:, :], gfacc[:, :], AF.Copy, scale=1.0 / S)
                st["ac_gf"] = inc(ins, "ac")
            A(_)

        w12cnt = [0, 0]
        w12free = [("pe", pe_phase1), ("pe", 0)]
        hidfree = [("pe", 0), ("pe", 0)]
        xloaded = {}
        w3cnt = 0
        w3free = ("pe", 0)
        h2free = ("pe", 0)
        pr8free = ("pe", 0)
        unit_idx = 0

        for q in range(NQRT):
            qoff = q * NQ
            if q > 0:
                def _(g):
                    k, v = w3free
                    w(g, k, v)
                    ins = g.dma_start(w3wq[:, :], hwq)
                    inc(ins, "c")
                G(_)
            wq_wait = ("c", st["c_init"] if q == 0 else C["c"])

            at = [{}, {}]

            def s1(t):
                par = (q * 2 + t) % 2
                skey = f"x{par}"
                n0 = qoff + t * NT
                if (q, t) not in xloaded:
                    def _(g, par=par, skey=skey, n0=n0):
                        k, v = xfree[par]
                        w(g, k, v)
                        ins = g.dma_start(xb[par][:, :], hx[n0 // NT])
                        inc(ins, skey)
                    G(_)
                    xcnt[par] += 1
                    xloaded[(q, t)] = 16 * xcnt[par]
                xw = [(skey, xloaded[(q, t)])]
                xcol = (lambda par: lambda c: xb[par][:, c * NT:(c + 1) * NT])(par)
                hcol = (lambda par: lambda c: hb[par][:, c * NT:(c + 1) * NT])(par)
                at[t]["dv_h"] = emit_ln(par, xcol, xw, hcol, [hfree[par]],
                                        flat_src=xb[par][:, :])

            def s2a(t):
                par = (q * 2 + t) % 2
                qsq_cnt = {}

                def qevac(c, bn, bk, pe_cnt, par=par, t=t):
                    def _(a, c=c, bk=bk):
                        w(a, "pe", pe_cnt)
                        ins = a.activation(sq[par][:, c * NT:(c + 1) * NT], bk[:, :], AF.Square)
                        qsq_cnt[c] = inc(ins, "ac")
                    A(_)

                    def _(v, c=c, bk=bk, t=t):
                        w(v, "pe", pe_cnt)
                        if c == 0 and t == 0:
                            w(v, "out", 16 * q)
                        ins = v.tensor_copy(
                            ysb[:, c * NQ + t * NT:c * NQ + (t + 1) * NT], bk[:, :])
                        st["dv_qrs"] = inc(ins, "dv")
                    V(_)
                    bank_read(bn, "ac", qsq_cnt[c])
                    bank_read(bn, "dv", st["dv_qrs"])

                emit_proj(w3wq, hb[par], at[t]["dv_h"], qevac, extra_pe_wait=[wq_wait])
                hfree[par] = ("pe", st["pe_proj"])
                st["pe_qproj"] = st["pe_proj"]
                RD = 1536 if t == 0 else 0
                rfree = rowfreeL2 if t == 0 else rowfreeLN
                pe_k, ac_k = emit_l2a(par, qsq_cnt[NCH - 1], RD, rfree)
                at[t]["ac_k"] = ac_k
                at[t]["pe_ksum"] = pe_k

            def s2b(t):
                par = (q * 2 + t) % 2
                RD = 1536 if t == 0 else 0
                RE = 2048 if t == 0 else 512
                rfree = rowfreeL2 if t == 0 else rowfreeLN
                rboff = 1024 if t == 0 else 0
                ac_rsqbS = emit_l2b(par, at[t]["ac_k"], RD, RE, rfree, rboff)

                def _(v, par=par, t=t, q=q, ac_rsqbS=ac_rsqbS):
                    w(v, "ac", ac_rsqbS)
                    w(v, "ac", st["ac_gf"])
                    for c in range(NCH):
                        ys = ysb[:, c * NQ + t * NT:c * NQ + (t + 1) * NT]
                        v.tensor_tensor(ys, ys, rsqbS[:, :], ALU.mult)
                        ins = v.scalar_tensor_tensor(
                            ys, ys, gfsb[:, c:c + 1],
                            xb[par][:, c * NT:(c + 1) * NT], ALU.mult, ALU.add)
                    st["dv_y"] = inc(ins, "dv")
                V(_)
                st["dv_rsqbS_reads"] = st["dv_y"]
                xfree[par] = ("dv", st["dv_y"])
                at[t]["dv_y"] = st["dv_y"]
                sqfree[par] = [("pe", at[t]["pe_ksum"])]

            def s3(t):
                par = (q * 2 + t) % 2
                ycol = (lambda t: lambda c: ysb[:, c * NQ + t * NT:c * NQ + (t + 1) * NT])(t)
                h2col = (lambda t: lambda c: h2sb[:, c * NQ + t * NT:c * NQ + (t + 1) * NT])(t)
                at[t]["dv_h2"] = emit_ln(par, ycol, [("dv", at[t]["dv_y"])], h2col, [h2free])
                sqfree[par] = [("pe", st["pe_stats"])]

            def s4(t):
                def _(te, t=t):
                    w(te, "dv", at[t]["dv_h2"])
                    bank_wait(te, "pb7")
                    for g in range(4):
                        for c in range(NCH):
                            ins = te.matmul(
                                pb7[:, t * 32 + g * 8:t * 32 + g * 8 + 8],
                                h2sb[:, c * NQ + t * NT + g * 128:c * NQ + t * NT + (g + 1) * 128],
                                gatesb[:, c * NE:(c + 1) * NE],
                                start=(c == 0), stop=(c == NCH - 1))
                    st["pe_scores"] = inc(ins, "pe")
                T(_)
                pe_scores = st["pe_scores"]

                for g in range(4):
                    def _(v, t=t, g=g):
                        w(v, "pe", pe_scores)
                        sc = pb7[:, t * 32 + g * 8:t * 32 + g * 8 + 8]
                        ins = v.reduce_max(colsc[:, 0:1], sc, AX.X)
                        cnt = inc(ins, "dv")
                        v.wait_ge(SEM["dv"], cnt)
                        ins = v.tensor_scalar(scr8[:, 0:8], sc, colsc[:, 0:1], None, ALU.is_equal)
                        cnt = inc(ins, "dv")
                        v.wait_ge(SEM["dv"], cnt)
                        ins = v.scalar_tensor_tensor(scr8[:, 8:16], scr8[:, 0:8], -1e30,
                                                     sc, ALU.mult, ALU.add)
                        cnt = inc(ins, "dv")
                        v.wait_ge(SEM["dv"], cnt)
                        ins = v.reduce_max(colsc[:, 1:2], scr8[:, 8:16], AX.X)
                        cnt = inc(ins, "dv")
                        v.wait_ge(SEM["dv"], cnt)
                        ins = v.tensor_tensor(colsc[:, 2:3], colsc[:, 0:1], colsc[:, 1:2],
                                              ALU.subtract)
                        st["dv_d"] = inc(ins, "dv")
                    V(_)

                    def _(a):
                        w(a, "dv", st["dv_d"])
                        ins = a.activation(colsc[:, 3:4], colsc[:, 2:3], AF.Sigmoid)
                        st["ac_p1"] = inc(ins, "ac")
                    A(_)

                    def _(v, t=t, g=g):
                        w(v, "ac", st["ac_p1"])
                        ins = v.tensor_scalar(colsc[:, 4:5], colsc[:, 3:4], -1.0, 1.0,
                                              ALU.mult, ALU.add)
                        cnt = inc(ins, "dv")
                        v.wait_ge(SEM["dv"], cnt)
                        ins = v.tensor_scalar(scr8[:, 16:24], scr8[:, 8:16], colsc[:, 1:2],
                                              None, ALU.is_equal)
                        cnt = inc(ins, "dv")
                        v.wait_ge(SEM["dv"], cnt)
                        ins = v.tensor_scalar(scr8[:, 24:32], scr8[:, 0:8], colsc[:, 3:4],
                                              None, ALU.mult)
                        cnt = inc(ins, "dv")
                        v.wait_ge(SEM["dv"], cnt)
                        ins = v.tensor_scalar(scr8[:, 16:24], scr8[:, 16:24], colsc[:, 4:5],
                                              None, ALU.mult)
                        cnt = inc(ins, "dv")
                        v.wait_ge(SEM["dv"], cnt)
                        ins = v.tensor_tensor(probs[:, t * 32 + g * 8:t * 32 + g * 8 + 8],
                                              scr8[:, 24:32], scr8[:, 16:24], ALU.add)
                        st["dv_probs"] = inc(ins, "dv")
                    V(_)
                    bank_read("pb7", "dv", st["dv_probs"])

                    def _(te, t=t, g=g):
                        bank_wait(te, "pb7")
                        w(te, "dv", st["dv_consts"])
                        k, v = pr8free
                        w(te, k, v)
                        ins = te.transpose(ptr, probs[:, t * 32 + g * 8:t * 32 + g * 8 + 8],
                                           ident[:, :])
                        st["pe_tr"] = inc(ins, "pe")
                    T(_)

                    def _(v, t=t, g=g):
                        w(v, "pe", st["pe_tr"])
                        ins = v.tensor_copy(pr8[0:8, t * NT + g * 128:t * NT + (g + 1) * 128], ptr)
                        st["dv_pr8"] = inc(ins, "dv")
                    V(_)
                    bank_read("pb7", "dv", st["dv_pr8"])

            s1(0); s1(1); s2a(0); s2a(1)
            if q == 0:
                emit_gf_finalize()
            s2b(0); s2b(1)
            s3(0); s3(1); s4(0); s4(1)
            st["dv_h2_1"] = at[1]["dv_h2"]
            st["pe_proj"] = st["pe_qproj"]

            dv_h2_all = st["dv_h2_1"]
            dv_pr8_all = st["dv_pr8"]
            w3free = ("pe", st["pe_proj"])

            if q + 1 < NQRT:
                for t2 in range(2):
                    par2 = ((q + 1) * 2 + t2) % 2
                    skey2 = f"x{par2}"
                    n02 = (q + 1) * NQ + t2 * NT

                    def _(g, par2=par2, skey2=skey2, n02=n02):
                        k, v = xfree[par2]
                        w(g, k, v)
                        ins = g.dma_start(xb[par2][:, :], hx[n02 // NT])
                        inc(ins, skey2)
                    G(_)
                    xcnt[par2] += 1
                    xloaded[(q + 1, t2)] = 16 * xcnt[par2]

            for e in range(NE):
                for t in range(2):
                    def _(te, e=e, t=t):
                        w(te, "dv", dv_pr8_all)
                        bank_wait(te, "pb3")
                        ins = te.matmul(pb3[:, :], onehot[:, e * 128:(e + 1) * 128],
                                        pr8[0:8, t * NT:(t + 1) * NT], start=True, stop=True)
                        st["pe_pb"] = inc(ins, "pe")
                    T(_)

                    def _(v, t=t):
                        w(v, "pe", st["pe_pb"])
                        ins = v.tensor_copy(pbS[:, t * NT:(t + 1) * NT], pb3[:, :])
                        st["dv_pb"] = inc(ins, "dv")
                    V(_)
                    bank_read("pb3", "dv", st["dv_pb"])
                dv_pb = st["dv_pb"]
                pe_pb_last = st["pe_pb"]

                for hh in range(2):
                    slot = unit_idx % 2
                    wkey = f"w{slot}"

                    def _(g, e=e, hh=hh, slot=slot, wkey=wkey):
                        k, v = w12free[slot]
                        w(g, k, v)
                        ins = g.dma_start(w12sb[slot], hw12[e, hh])
                        inc(ins, wkey)
                        k, v = w3free
                        w(g, k, v)
                        ins = g.dma_start(w3wq[:, :], hw3[e, hh])
                        inc(ins, "w3")
                    G(_)
                    w12cnt[slot] += 1
                    w3cnt += 1

                    hid_dv_t = [{}, {}]
                    for t in range(2):
                        hid_dv = hid_dv_t[t]
                        for j in range(8):
                            bU1n, bU2n = ("pb1", "pb2") if j % 2 == 0 else ("pb3", "pb4")
                            bU1, bU2 = PB[bU1n], PB[bU2n]

                            def _(te, t=t, j=j, slot=slot, wkey=wkey,
                                  bU1n=bU1n, bU2n=bU2n, bU1=bU1, bU2=bU2):
                                if j == 0 and t == 0:
                                    w(te, wkey, 16 * w12cnt[slot])
                                    w(te, "dv", max(dv_h2_all, dv_pb))
                                bank_wait(te, bU1n)
                                for k in range(NCH):
                                    ins = te.matmul(
                                        bU1[:, :],
                                        w12sb[slot][:, k * 2048 + j * 128:k * 2048 + (j + 1) * 128],
                                        h2sb[:, k * NQ + t * NT:k * NQ + (t + 1) * NT],
                                        start=(k == 0), stop=(k == NCH - 1))
                                cnt1 = inc(ins, "pe")
                                bank_wait(te, bU2n)
                                for k in range(NCH):
                                    ins = te.matmul(
                                        bU2[:, :],
                                        w12sb[slot][:, k * 2048 + (8 + j) * 128:k * 2048 + (9 + j) * 128],
                                        h2sb[:, k * NQ + t * NT:k * NQ + (t + 1) * NT],
                                        start=(k == 0), stop=(k == NCH - 1))
                                st["pe_u"] = (cnt1, inc(ins, "pe"))
                            T(_)
                            pe_u1, pe_u2 = st["pe_u"]

                            def _(a, j=j, bU1=bU1, pe_u1=pe_u1):
                                w(a, "pe", pe_u1)
                                w(a, "dv", hid_dv.get(j - 1, 0))
                                ins = a.activation(tmps[j % 2][:, :], bU1[:, :], AF.Silu)
                                st["ac_silu"] = inc(ins, "ac")
                            A(_)
                            bank_read(bU1n, "ac", st["ac_silu"])

                            def _(v, j=j, t=t, bU2=bU2, pe_u2=pe_u2):
                                w(v, "pe", pe_u2)
                                w(v, "ac", st["ac_silu"])
                                k_, v_ = hidfree[t]
                                w(v, k_, v_)
                                ins = v.tensor_tensor(hid[t][:, j * NT:(j + 1) * NT],
                                                      tmps[j % 2][:, :], bU2[:, :], ALU.mult)
                                hid_dv[j] = inc(ins, "dv")
                            V(_)
                            bank_read(bU2n, "dv", hid_dv[j])

                    for t in range(2):
                        dv_hid = hid_dv_t[t][7]
                        for m in range(NCH):
                            bEn = "pb5" if m % 2 == 0 else "pb6"
                            bE = PB[bEn]

                            def _(te, m=m, t=t, bEn=bEn, bE=bE, dv_hid=dv_hid):
                                if m == 0:
                                    w(te, "dv", dv_hid)
                                    w(te, "w3", 16 * w3cnt)
                                bank_wait(te, bEn)
                                for j in range(NCH):
                                    ins = te.matmul(
                                        bE[:, :],
                                        w3wq[:, j * 1024 + m * 128:j * 1024 + (m + 1) * 128],
                                        hid[t][:, j * NT:(j + 1) * NT],
                                        start=(j == 0), stop=(j == NCH - 1))
                                st["pe_eo"] = inc(ins, "pe")
                                if m == NCH - 1:
                                    hidfree[t] = ("pe", st["pe_eo"])
                            T(_)

                            def _(v, m=m, t=t, bE=bE):
                                w(v, "pe", st["pe_eo"])
                                v.tensor_tensor(tmpe[:, :], bE[:, :],
                                                pbS[:, t * NT:(t + 1) * NT], ALU.mult)
                                ins = v.tensor_tensor(
                                    ysb[:, m * NQ + t * NT:m * NQ + (t + 1) * NT],
                                    ysb[:, m * NQ + t * NT:m * NQ + (t + 1) * NT],
                                    tmpe[:, :], ALU.add)
                                st["dv_yadd"] = inc(ins, "dv")
                            V(_)
                            bank_read(bEn, "dv", st["dv_yadd"])

                    w12free[slot] = ("pe", st["pe_u"][1])
                    w3free = ("pe", st["pe_eo"])
                    unit_idx += 1

            h2free = ("pe", st["pe_u"][1])
            pr8free = ("pe", pe_pb_last)

            def _(g, q=q):
                w(g, "dv", st["dv_yadd"])
                ins = g.dma_start(
                    hout.rearrange("(m p) n -> p m n", p=128)[:, :, q * NQ:(q + 1) * NQ],
                    ysb[:].rearrange("p (m n) -> p m n", m=NCH))
                inc(ins, "out")
            G(_)

        def _(g):
            g.wait_ge(SEM["out"], C["out"])
        G(_)

        def _(sy):
            sy.wait_ge(SEM["out"], C["out"])
        block.sync(_)

    return nc


_NC_CACHE = {}


def kernel(L0_attn_x, L0_attn_W_q, L0_attn_W_k, L0_attn_W_v,
           L0_ffn_w12, L0_ffn_w3, L0_ffn_gate_w):
    x = np.asarray(L0_attn_x, dtype=np.float32)
    Wq = np.asarray(L0_attn_W_q, dtype=np.float32)
    Wk = np.asarray(L0_attn_W_k, dtype=np.float32)
    Wv = np.asarray(L0_attn_W_v, dtype=np.float32)
    w12 = np.asarray(L0_ffn_w12, dtype=np.float32)
    w3 = np.asarray(L0_ffn_w3, dtype=np.float32)
    gate = np.asarray(L0_ffn_gate_w, dtype=np.float32)

    B, Sq, Dm = x.shape
    xT = np.ascontiguousarray(x.transpose(0, 2, 1))

    def pmaj(a2d):
        k, rest = a2d.shape[0] // 128, a2d.shape[1]
        return np.ascontiguousarray(
            a2d.reshape(k, 128, rest).transpose(1, 0, 2).reshape(128, k * rest))

    WqT = pmaj(Wq.T)
    WkT = pmaj(Wk.T)
    WvT = pmaj(Wv.T)
    gateT = pmaj(np.ascontiguousarray(gate.T))
    w12h = np.empty((NE, 2, 128, NCH * 2048), np.float32)
    for e in range(NE):
        for hh in range(2):
            rws = np.concatenate([w12[e, hh * 1024:(hh + 1) * 1024],
                                  w12[e, 2048 + hh * 1024:2048 + (hh + 1) * 1024]], axis=0)
            w12h[e, hh] = pmaj(np.ascontiguousarray(rws.T))
    w3h = np.empty((NE, 2, 128, NCH * 1024), np.float32)
    for e in range(NE):
        for hh in range(2):
            w3h[e, hh] = pmaj(np.ascontiguousarray(w3[e][:, hh * 1024:(hh + 1) * 1024].T))

    in_maps = []
    for c in range(8):
        b, half = c // 2, c % 2
        own = xT[b][:, half * OWN:(half + 1) * OWN]
        xc = np.ascontiguousarray(
            own.reshape(NCH, 128, 8, NT).transpose(2, 1, 0, 3).reshape(8, 128, NCH * NT))
        in_maps.append({"x": xc, "wq": WqT, "wk": WkT, "wv": WvT, "gate": gateT,
                        "w12": w12h, "w3": w3h})

    if "nc" not in _NC_CACHE:
        _NC_CACHE["nc"] = build_nc()
    nc = _NC_CACHE["nc"]
    res = run_bass_kernel_spmd(nc, in_maps, core_ids=list(range(8)))

    out = np.empty((B, Sq, Dm), np.float32)
    for c in range(8):
        b, half = c // 2, c % 2
        out[b, half * OWN:(half + 1) * OWN] = res.results[c]["out"].T
    return out
